# version 02 (final)
# baseline (speedup 1.0000x reference)
import numpy as np

import concourse.bass as bass
import concourse.mybir as mybir
from concourse.bass_utils import run_bass_kernel_spmd

F32 = mybir.dt.float32
F16 = mybir.dt.float16
I32 = mybir.dt.int32
U8 = mybir.dt.uint8
AF = mybir.ActivationFunctionType
ALU = mybir.AluOpType
AX = mybir.AxisListType

T = 256
L = 4096
NT = L // T
D = 1024
NC = D // 128
HID = 4096
NH = HID // 128
CH = 1536
NCH = CH // 128
NTOK = T // 128
KTAP = 32
EPS = 1e-5
KB = 1024

ENGINES = ("sync", "tensor", "vector", "scalar", "gpsimd")


class Em:
    def __init__(self):
        self.ops = {e: [] for e in ENGINES}
        self.ctr = {}
        self.maxwait = {}

    def sem(self, name):
        self.ctr[name] = 0
        self.maxwait[name] = 0

    def c(self, name):
        return self.ctr[name]

    def emit(self, eng, fn=None, waits=(), incs=(), count=True):
        ws = []
        for s, v in waits:
            if v > 0:
                ws.append((s, v))
                self.maxwait[s] = max(self.maxwait[s], v)
        self.ops[eng].append((tuple(ws), fn, tuple(incs)))
        if count:
            for s, n in incs:
                self.ctr[s] += n

    def attach_inc(self, eng, incs):
        ws, fn, old = self.ops[eng][-1]
        assert fn is not None
        self.ops[eng][-1] = (ws, fn, tuple(old) + tuple(incs))
        for s, n in incs:
            self.ctr[s] += n

    def check(self):
        for s, v in self.maxwait.items():
            assert v <= self.ctr[s], f"sem {s}: wait {v} > total {self.ctr[s]}"


def build_program(debug=(), Lr=L, stop=12):
    NTr = Lr // T
    nc = bass.Bass("TRN2", target_bir_lowering=False,
                   detect_race_conditions=False)
    em = Em()
    for s in ("LD0", "LD1", "SC0", "SC1", "SA0", "SA1", "SB0", "SB1",
              "X0", "X1", "X2", "X3", "OWS0", "OWS1", "OWT", "OWO0", "OWO1",
              "GR", "MM", "VE", "AC"):
        em.sem(s)
    LDS = ("X0", "X1", "X2", "X3")

    def din(name, shape, dtype=F32):
        return nc.dram_tensor(name, list(shape), dtype, kind="ExternalInput").ap()

    xT = din("xT", [D, Lr])
    w12t = din("w12t", [NH, 128, 2 * D])
    h1t = din("h1t", [NH, 128, HID])
    h2t = din("h2t", [NH, 128, HID])
    w3t = din("w3t", [NC, 128, HID])
    whgt = din("whgt", [2 * NCH, 128, D])
    woutt = din("woutt", [NC, 128, CH])
    wlogt = din("wlogt", [128, NC * KTAP])
    blogr = din("blogr", [1, KTAP])
    w1s = din("w1s", [128, NH])
    w2s = din("w2s", [128, NH])
    h1sc = din("h1sc", [128, NH])
    h2sc = din("h2sc", [128, NH])
    w3sc = din("w3sc", [128, NC])
    bhg = din("bhg", [128, 2 * NCH])
    bout = din("bout", [128, NC])
    h0t = din("h0t", [128, NCH])
    l1w = din("l1w", [D, D])
    l1b = din("l1b", [128, NC])
    eye = din("eye", [128, 128])
    onescol = din("onescol", [128, 1])

    outT = nc.dram_tensor("outT", [D, Lr], F32, kind="ExternalOutput").ap()

    def dint(name, shape, dtype):
        return nc.dram_tensor(name, list(shape), dtype, kind="Internal").ap()

    w12hi = dint("w12hi", [NH, 128, 2 * D], F16)
    w12lo = dint("w12lo", [NH, 128, 2 * D], F16)
    h1hi = dint("h1hi", [NH, 128, HID], F16)
    h1lo = dint("h1lo", [NH, 128, HID], F16)
    h2hi = dint("h2hi", [NH, 128, HID], F16)
    h2lo = dint("h2lo", [NH, 128, HID], F16)
    w3hi = dint("w3hi", [NC, 128, HID], F16)
    w3lo = dint("w3lo", [NC, 128, HID], F16)
    whghi = dint("whghi", [2 * NCH, 128, D], F16)
    whglo = dint("whglo", [2 * NCH, 128, D], F16)
    wouthi = dint("wouthi", [NC, 128, CH], F16)
    woutlo = dint("woutlo", [NC, 128, CH], F16)
    wdTd = dint("wdTd", [NC, 128, D], F32)
    wTd = dint("wTd", [KTAP, T], F32)

    dbg = {}
    if "scr" in debug:
        dbg["scrD_d"] = nc.dram_tensor("dbg_scrD_d", [128, 512], F32,
                                       kind="ExternalOutput").ap()
        dbg["scrE_d"] = nc.dram_tensor("dbg_scrE_d", [128, 512], F32,
                                       kind="ExternalOutput").ap()
        dbg["wt32_d"] = nc.dram_tensor("dbg_wt32_d", [KTAP, T], F32,
                                       kind="ExternalOutput").ap()
    dbg_shapes = {"xn": [D, Lr], "mix": [D, Lr], "x1": [D, Lr], "xn2": [D, Lr],
                  "ho": [CH, Lr], "xd": [D, Lr], "hs": [HID, Lr], "g1": [HID, Lr],
                  "g2": [HID, Lr], "wsm": [KTAP, Lr], "acc2": [D, Lr],
                  "cc": [CH, Lr], "vv": [CH, Lr]}
    for name in debug:
        if name == "scr":
            continue
        dbg[name] = nc.dram_tensor("dbg_" + name, dbg_shapes[name], F32,
                                   kind="ExternalOutput").ap()

    S = nc.alloc_sbuf_tensor
    xb = [S(f"xb{i}", [128, NC * T], F32) for i in range(2)]
    xn = S("xn", [128, NC * (T + KTAP)], F32)
    xnhi = S("xnhi", [128, NC * T], F16)
    xnlo = S("xnlo", [128, NC * T], F16)
    mix = S("mix", [128, NC * T], F32)
    UA = S("UA", [128, 64 * KB], U8)
    ua16 = UA[:].bitcast(F16)
    ua32 = UA[:].bitcast(F32)
    uai = UA[:].bitcast(I32)

    def v16(off_b, nel):
        return ua16[:, off_b // 2:off_b // 2 + nel]

    def v32(off_b, nel):
        return ua32[:, off_b // 4:off_b // 4 + nel]

    hs_hi = v16(0, NH * T)
    hs_lo = v16(16 * KB, NH * T)
    g1_hi = v16(32 * KB, NH * T)
    g1_lo = v16(48 * KB, NH * T)
    hhat = v32(0, NCH * T)
    c_sb = v32(12 * KB, NCH * T)
    v_sb = v32(24 * KB, NCH * T)
    ho = v32(36 * KB, NCH * T)
    ho_hi = v16(48 * KB, NCH * T)
    ho_lo = v16(54 * KB, NCH * T)
    xtm = v32(0, NTOK * D)
    xd_tm = v32(8 * KB, NTOK * D)
    q1 = v32(16 * KB, D)
    q2 = v32(20 * KB, D)
    q3 = v32(24 * KB, D)
    q4 = v32(28 * KB, D)
    xd_fm = v32(32 * KB, NC * T)
    acc2 = v32(40 * KB, NC * T)
    p0in = [v32(0, HID), v32(16 * KB, HID)]
    p0hi = [v16(32 * KB, HID), v16(40 * KB, HID)]
    p0lo = [v16(48 * KB, HID), v16(56 * KB, HID)]
    qw_in = v32(0, D)
    qw_u = v32(4 * KB, D)
    qw_a = v32(8 * KB, D)
    qw_b = v32(12 * KB, D)
    qw_wd = v32(16 * KB, D)
    qw_c = v32(20 * KB, D)
    qw_st = v32(24 * KB, D)

    scrA = S("scrA", [128, 512], F32)
    scrB = S("scrB", [128, 512], F32)
    scrD = S("scrD", [128, 512], F32)
    scrE = S("scrE", [128, 512], F32)
    scrC = scrB
    scrF = scrD

    SAb = S("SAb", [128, 32 * KB], U8)
    SBb = S("SBb", [128, 32 * KB], U8)
    SCb = S("SCb", [128, 16 * KB], U8)
    wrow = S("wrow", [128, 8 * T], F32)
    WROW_BASE = {0: 0, 1: 32, 2: 64, 3: 0}
    wT32 = S("wT32", [KTAP, T], F32)

    wlog_hi = S("wlog_hi", [128, NC * KTAP], F16)
    wlog_lo = S("wlog_lo", [128, NC * KTAP], F16)
    blogbc = S("blogbc", [128, KTAP], F32)
    w1s_s = S("w1s_s", [128, NH], F32)
    w2s_s = S("w2s_s", [128, NH], F32)
    h1s_s = S("h1s_s", [128, NH], F32)
    h2s_s = S("h2s_s", [128, NH], F32)
    w3s_s = S("w3s_s", [128, NC], F32)
    bhg_s = S("bhg_s", [128, 2 * NCH], F32)
    nbhg_s = S("nbhg_s", [128, 2 * NCH], F32)
    bout_s = S("bout_s", [128, NC], F32)
    l1b_s = S("l1b_s", [128, NC], F32)
    h0_s = S("h0_s", [128, NCH], F32)
    st = S("st", [128, NCH], F32)
    eye_s = S("eye_s", [128, 128], F32)
    onesP = S("onesP", [128, 128], F32)
    ones1_s = onesP[0:1, 0:128]
    onescol_s = S("onescol_s", [128, 1], F32)
    qsc = S("qsc", [128, 32], F32)
    qrc = S("qrc", [128, 32], F32)
    nmax = S("nmax", [128, NTOK], F32)
    ssum = S("ssum", [128, NTOK], F32)
    rrec = S("rrec", [128, NTOK], F32)
    mrow_ = scrB[0:1, 0:T]
    r1_ = scrB[0:1, T:2 * T]
    r2_ = scrD[0:1, 0:T]
    tm0 = mrow_[0:1, 0:KTAP]
    tr1 = r1_[0:1, 0:KTAP]
    tr2 = r2_[0:1, 0:KTAP]
    wlogf = scrD[:, 0:NC * KTAP]
    blrow = scrD[0:1, T:T + KTAP]

    sbuf_left = nc.sbuf_bytes_remaining

    PS = [nc.alloc_psum_tensor(f"ps{i}", [128, 512], F32) for i in range(8)]
    REG = {
        "stats_sx": PS[0][0:1, 0:T],
        "stats_sq": PS[1][0:1, 0:T],
        "bc_mean": PS[2][:, 0:T],
        "bc_rstd": PS[2][:, T:2 * T],
        "wrepA": PS[2][:, 0:T],
        "bct_mean": PS[3][:, 0:KTAP],
        "bct_rstd": PS[3][:, KTAP:2 * KTAP],
        "wrepB": PS[3][:, 64:64 + T],
        "accA": PS[4][:, 0:T],
        "accA2": PS[4][:, T:2 * T],
        "accB": PS[5][:, 0:T],
        "accB2": PS[5][:, T:2 * T],
        "tpA": PS[6][:, 0:128],
        "log0": PS[6][:, 128:128 + KTAP],
        "wt0": PS[6][0:KTAP, 160:160 + 128],
        "stats_tx": PS[6][0:1, 288:288 + KTAP],
        "tpB": PS[7][:, 0:128],
        "log1": PS[7][:, 128:128 + KTAP],
        "wt1": PS[7][0:KTAP, 160:160 + 128],
        "stats_tq": PS[7][0:1, 288:288 + KTAP],
    }
    rfree = {k: ("MM", 0) for k in REG}

    def reg_w(*names):
        return [rfree[n] for n in names]

    def reg_done(name, sem="VE"):
        rfree[name] = (sem, em.c(sem))

    def mm(region, lhsT, rhs, start, stop, waits=()):
        r = REG[region]
        em.emit("tensor",
                (lambda eng, r=r, lh=lhsT, rh=rhs, s1=start, s2=stop:
                 eng.matmul(r, lh, rh, start=s1, stop=s2)),
                waits=waits)

    def mm_inc():
        em.attach_inc("tensor", [("MM", 1)])
        return em.c("MM")

    def tp(region, in_ap, waits=()):
        r = REG[region]
        em.emit("tensor",
                (lambda eng, r=r, i=in_ap: eng.transpose(r, i, eye_s[:])),
                waits=waits)
        return mm_inc()

    def ev(fn, waits=()):
        em.emit("vector", fn, waits=waits, incs=[("VE", 1)])
        return em.c("VE")

    def ea(fn, waits=()):
        em.emit("scalar", fn, waits=waits, incs=[("AC", 1)])
        return em.c("AC")

    streams = {
        "SC": {"base": SCb, "sb": 8 * KB, "ld": [], "cons": []},
        "SA": {"base": SAb, "sb": 16 * KB, "ld": [], "cons": []},
        "SB": {"base": SBb, "sb": 16 * KB, "ld": [], "cons": []},
    }
    dma_plan = []
    dma_emitted = [0]
    stream_extra_wait = {}
    x_plan = {}

    def plan_slice(arena, aps):
        stz = streams[arena]
        idx = len(stz["ld"])
        sem = arena + str(idx % 2)
        for _ in aps:
            em.ctr[sem] += 16
        stz["ld"].append((sem, em.c(sem)))
        dma_plan.append(("W", arena, idx, idx % 2, aps, sem))
        return idx % 2, (sem, em.c(sem))

    def plan_x(t):
        sem = LDS[t % 4]
        em.ctr[sem] += 16 * NC
        x_plan[t] = (sem, em.c(sem))
        dma_plan.append(("X", t, None, None, None, sem))
        return x_plan[t]

    def sl_view(arena, slot, off_b, nel, dtype):
        stz = streams[arena]
        tt_ = stz["base"][:].bitcast(dtype)
        es = 2 if dtype == F16 else 4
        o = (slot * stz["sb"] + off_b) // es
        return tt_[:, o:o + nel]

    def mark_consumed(arena):
        streams[arena]["cons"].append(("MM", em.c("MM")))

    out_ow = {}

    def emit_dmas():
        for i in range(dma_emitted[0], len(dma_plan)):
            kind = dma_plan[i][0]
            if kind == "X":
                t = dma_plan[i][1]
                sem = dma_plan[i][5]
                waits = []
                if t >= 2:
                    waits.append(out_ow[t - 2])
                for c in range(NC):
                    em.emit("sync",
                            (lambda eng, t=t, c=c: eng.dma_start(
                                out=xb[t % 2][:, c * T:(c + 1) * T],
                                in_=xT[c * 128:(c + 1) * 128, t * T:(t + 1) * T])),
                            waits=waits if c == 0 else (),
                            incs=[(sem, 16)], count=False)
            else:
                _, arena, idx, slot, aps, sem = dma_plan[i]
                stz = streams[arena]
                waits = []
                if idx >= 2:
                    if idx - 2 >= len(stz["cons"]):
                        continue
                    waits.append(stz["cons"][idx - 2])
                extra = stream_extra_wait.pop((arena, idx), None)
                if extra is not None:
                    waits.append(extra)
                first = True
                for ap_d, off_b, nel, dtype in aps:
                    dst = sl_view(arena, slot, off_b, nel, dtype)
                    em.emit("sync",
                            (lambda eng, d=dst, s=ap_d: eng.dma_start(out=d, in_=s)),
                            waits=waits if first else (), incs=[(sem, 16)],
                            count=False)
                    first = False
        dma_emitted[0] = len(dma_plan)

    def dump_fm(name, in_3d, t, wait_ve=None, wait_ac=None):
        if name not in dbg:
            return
        d = dbg[name]
        waits = []
        if wait_ve is not None:
            waits.append(("VE", wait_ve))
        if wait_ac is not None:
            waits.append(("AC", wait_ac))
        em.emit("gpsimd",
                (lambda eng, d=d, s=in_3d, t=t: eng.dma_start(
                    out=d.rearrange("(c p) l -> p c l", p=128)[:, :, t * T:(t + 1) * T],
                    in_=s)),
                waits=waits, incs=[("OWT", 16)])

    def fm3(ap2d, nch=NC, w=T):
        return ap2d.rearrange("p (c l) -> p c l", l=w)

    const_loads = [
        (w1s_s, w1s), (w2s_s, w2s), (h1s_s, h1sc),
        (h2s_s, h2sc), (w3s_s, w3sc), (bhg_s, bhg), (bout_s, bout),
        (h0_s, h0t), (l1b_s, l1b), (eye_s, eye),
        (onescol_s, onescol),
    ]
    for dst, src in const_loads:
        em.emit("sync", (lambda eng, d=dst, s=src: eng.dma_start(out=d[:], in_=s)),
                incs=[("LD0", 16)])
    em.emit("sync", (lambda eng: eng.dma_start(out=wlogf, in_=wlogt)),
            incs=[("LD0", 16)])
    em.emit("sync", (lambda eng: eng.dma_start(out=blrow, in_=blogr)),
            incs=[("LD0", 16)])
    ld_consts = em.c("LD0")
    plan_x(0)
    emit_dmas()

    ev(lambda eng: eng.memset(onesP[:], 1.0))
    ev(lambda eng: eng.tensor_copy(wlog_hi[:], wlogf), waits=[("LD0", ld_consts)])
    ev(lambda eng: eng.tensor_sub(wlog_lo[:], wlogf, wlog_hi[:]))
    ev(lambda eng: eng.tensor_single_scalar(nbhg_s[:], bhg_s[:], -1.0, ALU.mult))
    ev(lambda eng: eng.tensor_scalar(scrB[:, 0:NCH], h0_s[:], 0.0, 0.0,
                                     ALU.add, ALU.is_ge))
    ev(lambda eng: eng.tensor_single_scalar(scrB[:, 32:32 + NCH], h0_s[:], 0.5, ALU.add))
    ac_g0 = ea(lambda eng: eng.activation(scrA[:, 0:NCH], h0_s[:], AF.Sigmoid),
               waits=[("LD0", ld_consts)])
    ev(lambda eng: eng.tensor_sub(scrB[:, 32:32 + NCH], scrB[:, 32:32 + NCH],
                                  scrA[:, 0:NCH]),
       waits=[("AC", ac_g0)])
    ev(lambda eng: eng.tensor_mul(scrB[:, 0:NCH], scrB[:, 0:NCH],
                                  scrB[:, 32:32 + NCH]))
    ev(lambda eng: eng.tensor_add(st[:], scrB[:, 0:NCH], scrA[:, 0:NCH]))
    mm("log0", ones1_s, blrow, True, True,
       waits=[("LD0", ld_consts), ("VE", em.c("VE"))] + reg_w("log0"))
    mmv = mm_inc()
    ev(lambda eng: eng.tensor_copy(blogbc[:], REG["log0"]), waits=[("MM", mmv)])
    reg_done("log0")

    def emit_quant(src_ap, dst_ap, b_u, b_a, b_b, b_c, waits=()):
        v3 = src_ap.rearrange("p (nb b) -> p nb b", b=32)
        ve_p = ev(lambda eng: eng.reduce_max(qsc[:], v3, axis=AX.X,
                                             apply_absolute_value=True),
                  waits=waits)
        ev(lambda eng: eng.tensor_single_scalar(qsc[:], qsc[:], float(1 / 7.5),
                                                ALU.mult), waits=[("VE", ve_p)])
        ve_p = ev(lambda eng: eng.tensor_scalar_max(qsc[:], qsc[:], 1e-30))
        ve_p = ev(lambda eng: eng.reciprocal(qrc[:], qsc[:]),
                  waits=[("VE", ve_p)])
        rcb = qrc[:].broadcast_to((128, 32, 32))
        scb = qsc[:].broadcast_to((128, 32, 32))
        u3 = b_u.rearrange("p (nb b) -> p nb b", b=32)
        ev(lambda eng: eng.tensor_mul(u3, v3, rcb), waits=[("VE", ve_p)])
        ev(lambda eng: eng.tensor_single_scalar(b_a, b_u, -1.0, ALU.mult))
        ev(lambda eng: eng.tensor_max(b_a, b_a, b_u))
        ev(lambda eng: eng.tensor_scalar(b_b, b_a, 0.0, 2.0, ALU.add, ALU.is_ge))
        ev(lambda eng: eng.tensor_scalar(b_a, b_a, 0.0, 4.0, ALU.add, ALU.is_ge))
        ev(lambda eng: eng.tensor_scalar(b_c, b_a, -2.0, 8.0, ALU.mult, ALU.add))
        ev(lambda eng: eng.scalar_tensor_tensor(b_c, b_b, -4.0, b_c,
                                                ALU.mult, ALU.add))
        ev(lambda eng: eng.tensor_mul(b_u, b_u, b_c))
        ev(lambda eng: eng.tensor_scalar(b_a, b_a, 0.25, 0.125, ALU.mult, ALU.add))
        ev(lambda eng: eng.scalar_tensor_tensor(b_a, b_b, 0.125, b_a,
                                                ALU.mult, ALU.add))
        ev(lambda eng: eng.tensor_scalar(b_c, b_u, 0.0, 0.0, ALU.add, ALU.is_ge))
        ev(lambda eng: eng.tensor_scalar(b_c, b_c, 2.0, -1.0, ALU.mult, ALU.add))
        ev(lambda eng: eng.tensor_single_scalar(b_b, b_u, -1.0, ALU.mult))
        ev(lambda eng: eng.tensor_max(b_b, b_b, b_u))
        for i_, step in enumerate((16.0, 8.0, 4.0, 2.0, 1.0)):
            ev((lambda eng, step=step: eng.tensor_scalar(
                dst_ap, b_b, 0.0, step - 0.5, ALU.add, ALU.is_ge)))
            if i_ == 0:
                ev((lambda eng, step=step: eng.tensor_single_scalar(
                    b_u, dst_ap, step, ALU.mult)))
            else:
                ev((lambda eng, step=step: eng.scalar_tensor_tensor(
                    b_u, dst_ap, step, b_u, ALU.mult, ALU.add)))
            ev((lambda eng, step=step: eng.scalar_tensor_tensor(
                b_b, dst_ap, -step, b_b, ALU.mult, ALU.add)))
        ev(lambda eng: eng.tensor_mul(b_u, b_u, b_c))
        ev(lambda eng: eng.tensor_mul(b_u, b_u, b_a))
        u3b = b_u.rearrange("p (nb b) -> p nb b", b=32)
        d3 = dst_ap.rearrange("p (nb b) -> p nb b", b=32)
        return ev(lambda eng: eng.tensor_mul(d3, u3b, scb))

    wdT_ow = []
    prev_tp_ve = 0
    for o in range(NC):
        waits = [("LD0", ld_consts)]
        if o > 0:
            waits.append(("VE", prev_tp_ve))
        em.emit("sync", (lambda eng, o=o: eng.dma_start(
            out=qw_in, in_=l1w[o * 128:(o + 1) * 128, :])),
            waits=waits, incs=[("LD0", 16)])
        ldv = em.c("LD0")
        ve_q = emit_quant(qw_in, qw_wd, qw_u, qw_a, qw_b, qw_c,
                          waits=[("LD0", ldv)])
        for k in range(NC):
            r = "tpA" if k % 2 == 0 else "tpB"
            mmw = tp(r, qw_wd[:, k * 128:(k + 1) * 128],
                     waits=[("VE", ve_q)] + reg_w(r))
            w_st = [("MM", mmw)]
            if k == 0 and o > 0:
                w_st.append(wdT_ow[o - 1])
            prev_tp_ve = ev(
                (lambda eng, r=r, k=k: eng.tensor_copy(
                    qw_st[:, k * 128:(k + 1) * 128], REG[r])),
                waits=w_st)
            reg_done(r)
        ow_wait = [wdT_ow[o - 1]] if o > 0 else []
        em.emit("gpsimd", (lambda eng, o=o: eng.dma_start(out=wdTd[o], in_=qw_st)),
                waits=[("VE", prev_tp_ve)] + ow_wait, incs=[("OWT", 16)])
        wdT_ow.append(("OWT", em.c("OWT")))
    ve_quant_done = prev_tp_ve
    ow_quant_done = em.c("OWT")

    split_jobs = [
        ("w12", w12t, w12hi, w12lo, NH, 2 * D),
        ("h1", h1t, h1hi, h1lo, NH, HID),
        ("h2", h2t, h2hi, h2lo, NH, HID),
        ("w3", w3t, w3hi, w3lo, NC, HID),
        ("whg", whgt, whghi, whglo, 2 * NCH, D),
        ("wout", woutt, wouthi, woutlo, NC, CH),
    ]
    split_ow = {}
    p0_hist = []
    for name, src, dhi, dlo, nchunks, ln in split_jobs:
        split_ow[name] = []
        for o in range(nchunks):
            i = len(p0_hist)
            slot = i % 2
            lsem = "LD" + str(i % 2)
            osem = "OWS" + str(i % 2)
            waits = []
            if i >= 2:
                _, ac_p, ve_p, _ = p0_hist[i - 2]
                waits = [("AC", ac_p), ("VE", ve_p)]
            else:
                waits = [("VE", ve_quant_done), ("OWT", ow_quant_done)]
            em.emit("sync", (lambda eng, s=src, o=o, sl=slot, n=ln:
                             eng.dma_start(out=p0in[sl][:, 0:n], in_=s[o])),
                    waits=waits, incs=[(lsem, 16)])
            ldv = em.c(lsem)
            ow_wait = [p0_hist[i - 2][3]] if i >= 2 else []
            ac_hi = ea(lambda eng, sl=slot, n=ln: eng.copy(p0hi[sl][:, 0:n],
                                                           p0in[sl][:, 0:n]),
                       waits=[(lsem, ldv)] + ow_wait)
            ve_lo = ev(lambda eng, sl=slot, n=ln: eng.tensor_sub(
                p0lo[sl][:, 0:n], p0in[sl][:, 0:n], p0hi[sl][:, 0:n]),
                waits=[("AC", ac_hi)] + ow_wait)
            em.emit("gpsimd", (lambda eng, d=dhi, o=o, sl=slot, n=ln:
                               eng.dma_start(out=d[o], in_=p0hi[sl][:, 0:n])),
                    waits=[("AC", ac_hi)], incs=[(osem, 16)])
            em.emit("gpsimd", (lambda eng, d=dlo, o=o, sl=slot, n=ln:
                               eng.dma_start(out=d[o], in_=p0lo[sl][:, 0:n])),
                    waits=[("VE", ve_lo)], incs=[(osem, 16)])
            p0_hist.append((ldv, ac_hi, ve_lo, (osem, em.c(osem))))
            split_ow[name].append((osem, em.c(osem)))
    ve_split_done = p0_hist[-1][2]
    ac_split_done = p0_hist[-1][1]
    ow_splits_final = [("OWS0", em.c("OWS0")), ("OWS1", em.c("OWS1")),
                      ("OWT", em.c("OWT"))]

    scrE_free = {0: ("MM", 0), 1: ("MM", 0)}

    def emit_layernorm(t, x_ap, out_xn, xchunk_ready, stats=("stats_sx", "stats_sq"),
                       bc=("bc_mean", "bc_rstd"), width=T, xoff=0, xn_stride=T + KTAP,
                       xn_off=0, do_split=True):
        sqmm = {}
        sqs = []
        for c in range(NC):
            w_ = list(xchunk_ready(c))
            w_.append(scrE_free[c % 2] if c < 2 else ("MM", sqmm[c - 2]))
            ac = ea((lambda eng, c=c: eng.activation(
                scrE[:, (c % 2) * T:(c % 2) * T + width],
                x_ap[:, c * T + xoff:c * T + xoff + width], AF.Square)), waits=w_)
            sqs.append(ac)
            w2_ = [("AC", ac)]
            if c == 0:
                w2_ += reg_w(stats[1])
            mm(stats[1], onescol_s[:], scrE[:, (c % 2) * T:(c % 2) * T + width],
               c == 0, c == NC - 1, waits=w2_)
            sqmm[c] = mm_inc()
        mm_sq = sqmm[NC - 1]
        for c in range(NC):
            w_ = list(xchunk_ready(c))
            if c == 0:
                w_ += reg_w(stats[0])
            mm(stats[0], onescol_s[:], x_ap[:, c * T + xoff:c * T + xoff + width],
               c == 0, c == NC - 1, waits=w_)
        mm_sx = mm_inc()
        for c in range(NC):
            scrE_free[c % 2] = ("MM", mm_sx)
        mr, rr1, rr2 = ((mrow_, r1_, r2_) if width == T
                        else (tm0, tr1, tr2))
        ve_c = ev(lambda eng: eng.tensor_single_scalar(mr, REG[stats[0]], 1.0 / D,
                                                       ALU.mult),
                  waits=[("MM", mm_sx)])
        ev(lambda eng: eng.tensor_single_scalar(rr1, REG[stats[1]], 1.0 / D, ALU.mult))
        reg_done(stats[0])
        reg_done(stats[1])
        ev(lambda eng: eng.tensor_mul(rr2, mr, mr), waits=[("VE", ve_c)])
        ve_c = em.c("VE")
        ev(lambda eng: eng.scalar_tensor_tensor(rr1, rr2, -1.0, rr1,
                                                ALU.mult, ALU.add),
           waits=[("VE", ve_c)])
        ve_v = ev(lambda eng: eng.tensor_single_scalar(rr1, rr1, EPS, ALU.add))
        ac1 = ea(lambda eng: eng.activation(rr2, rr1, AF.Ln), waits=[("VE", ve_v)])
        ac2 = ea(lambda eng: eng.activation(rr1, rr2, AF.Exp, scale=-0.5))
        mm(bc[0], ones1_s, mr, True, True, waits=[("AC", ac2)] + reg_w(bc[0]))
        mm_inc()
        mm(bc[1], ones1_s, rr1, True, True, waits=reg_w(bc[1]))
        mm_bc = mm_inc()
        split_ve = []
        ve_last = 0
        for c in range(NC):
            sl = scrF[:, (c % 2) * T:(c % 2) * T + width]
            ev((lambda eng, c=c, sl=sl: eng.tensor_tensor(
                sl, x_ap[:, c * T + xoff:c * T + xoff + width],
                REG[bc[0]], ALU.subtract)),
               waits=[("MM", mm_bc)] if c == 0 else ())
            ve_last = ev((lambda eng, c=c, sl=sl: eng.tensor_tensor(
                out_xn[:, c * xn_stride + xn_off:c * xn_stride + xn_off + width],
                sl, REG[bc[1]], ALU.mult)))
            if do_split:
                ev((lambda eng, c=c: eng.tensor_copy(
                    xnhi[:, c * T:c * T + width],
                    out_xn[:, c * xn_stride + xn_off:c * xn_stride + xn_off + width])))
                ve_last = ev((lambda eng, c=c: eng.tensor_sub(
                    xnlo[:, c * T:c * T + width],
                    out_xn[:, c * xn_stride + xn_off:c * xn_stride + xn_off + width],
                    xnhi[:, c * T:c * T + width])))
                split_ve.append(ve_last)
        reg_done(bc[0])
        reg_done(bc[1])
        return ve_last, split_ve

    gr_wrow = 0
    ve_final = 0
    mm_wrep_prev = 0

    for t in range(NTr):
        if t + 1 < NTr:
            plan_x(t + 1)
        sc_info = []
        for o in range(NH):
            sc_info.append(plan_slice("SC", [
                (w12hi[o], 0, 2 * D, F16), (w12lo[o], 4 * KB, 2 * D, F16)]))
        sa_info = []
        for o in range(NH):
            sa_info.append(plan_slice("SA", [
                (h1hi[o], 0, HID, F16), (h1lo[o], 8 * KB, HID, F16)]))
        sb_info = []
        for o in range(NH):
            sb_info.append(plan_slice("SB", [
                (h2hi[o], 0, HID, F16), (h2lo[o], 8 * KB, HID, F16)]))
        for o in range(NC):
            sa_info.append(plan_slice("SA", [
                (w3hi[o], 0, HID, F16), (w3lo[o], 8 * KB, HID, F16)]))
        for o in range(2 * NCH):
            sa_info.append(plan_slice("SA", [
                (whghi[o], 0, D, F16), (whglo[o], 2 * KB, D, F16)]))
        for o in range(NC):
            sa_info.append(plan_slice("SA", [
                (wouthi[o], 0, CH, F16), (woutlo[o], 3 * KB, CH, F16)]))
        for o in range(NC):
            sa_info.append(plan_slice("SA", [(wdTd[o], 0, D, F32)]))
        if t == 0:
            for o in range(NH):
                stream_extra_wait[("SC", o)] = split_ow["w12"][o]
                stream_extra_wait[("SA", o)] = split_ow["h1"][o]
                stream_extra_wait[("SB", o)] = split_ow["h2"][o]
            for o in range(NC):
                stream_extra_wait[("SA", NH + o)] = split_ow["w3"][o]
            for o in range(2 * NCH):
                stream_extra_wait[("SA", NH + NC + o)] = split_ow["whg"][o]
            for o in range(NC):
                stream_extra_wait[("SA", NH + NC + 2 * NCH + o)] = \
                    split_ow["wout"][o]
            for o in range(NC):
                stream_extra_wait[("SA", NH + 2 * NC + 2 * NCH + o)] = \
                    wdT_ow[o]

        sa_i = iter(sa_info)

        if stop < 1:
            emit_dmas()
            continue
        if t == 0:
            def xrd(c, t=t):
                return [x_plan[t], ("VE", ve_split_done), ("AC", ac_split_done)]
        else:
            def xrd(c, t=t):
                return [x_plan[t]]
        _, xn_split_ve = emit_layernorm(t, xb[t % 2][:], xn[:], xrd)
        ve_xn = xn_split_ve[-1]
        dump_fm("xn", fm3(xn[:], w=T + KTAP)[:, :, 0:T], t, wait_ve=ve_xn)

        if stop < 2:
            emit_dmas()
            continue
        wsm_ve = []
        wt_mm = []
        for m_ in range(NTOK):
            lreg = "log0" if m_ == 0 else "log1"
            first = True
            for k in range(NC):
                lh_hi = xnhi[:, k * T + m_ * 128:k * T + m_ * 128 + 128]
                lh_lo = xnlo[:, k * T + m_ * 128:k * T + m_ * 128 + 128]
                rh_hi = wlog_hi[:, k * KTAP:(k + 1) * KTAP]
                rh_lo = wlog_lo[:, k * KTAP:(k + 1) * KTAP]
                w_ = ([("VE", ve_xn)] + reg_w(lreg)) if first else ()
                mm(lreg, lh_hi, rh_hi, k == 0, False, waits=w_)
                mm(lreg, lh_hi, rh_lo, False, False)
                mm(lreg, lh_lo, rh_hi, False, k == NC - 1)
                first = False
            mm_log = mm_inc()
            lg = scrD[:, T + m_ * 32:T + (m_ + 1) * 32]
            ev((lambda eng, lg=lg, lreg=lreg: eng.tensor_tensor(
                lg, REG[lreg], blogbc[:], ALU.add)), waits=[("MM", mm_log)])
            reg_done(lreg)
            ev((lambda eng, lg=lg, m_=m_: eng.reduce_max(
                nmax[:, m_:m_ + 1], lg, axis=AX.X, negate=True)))
            ve_n = em.c("VE")
            e_ = scrD[:, 320 + m_ * 32:320 + (m_ + 1) * 32]
            ac_e = ea((lambda eng, e_=e_, lg=lg, m_=m_: eng.activation(
                e_, lg, AF.Exp, bias=nmax[:, m_:m_ + 1])), waits=[("VE", ve_n)])
            ve_s = ev((lambda eng, e_=e_, m_=m_: eng.reduce_sum(
                ssum[:, m_:m_ + 1], e_, axis=AX.X)), waits=[("AC", ac_e)])
            ve_r = ev((lambda eng, m_=m_: eng.reciprocal(rrec[:, m_:m_ + 1],
                                                         ssum[:, m_:m_ + 1])),
                      waits=[("VE", ve_s)])
            wv = scrE[:, m_ * 32:(m_ + 1) * 32]
            ve_w = ev((lambda eng, wv=wv, e_=e_, m_=m_: eng.tensor_scalar(
                wv, e_, rrec[:, m_:m_ + 1], None, ALU.mult)),
                waits=[("VE", ve_r)] + ([scrE_free[0]] if m_ == 0 else []))
            wsm_ve.append(ve_w)
            wreg = "wt0" if m_ == 0 else "wt1"
            mmw = tp(wreg, wv, waits=[("VE", ve_w)] + reg_w(wreg))
            wt_mm.append(mmw)
            ve_wt = ev((lambda eng, m_=m_, wreg=wreg: eng.tensor_copy(
                wT32[:, m_ * 128:(m_ + 1) * 128], REG[wreg])), waits=[("MM", mmw)])
            reg_done(wreg)
        scrE_free[0] = ("MM", max(wt_mm))
        scrE_free[1] = ("MM", max(wt_mm))
        if "scr" in debug and t == 0:
            em.emit("gpsimd", (lambda eng: eng.dma_start(
                out=dbg["scrD_d"], in_=scrD[:])),
                waits=[("VE", em.c("VE")), ("AC", em.c("AC"))], incs=[("OWT", 16)])
            em.emit("gpsimd", (lambda eng: eng.dma_start(
                out=dbg["scrE_d"], in_=scrE[:])), incs=[("OWT", 16)])
            em.emit("gpsimd", (lambda eng: eng.dma_start(
                out=dbg["wt32_d"], in_=wT32[:])), incs=[("OWT", 16)])
        em.emit("gpsimd", (lambda eng: eng.dma_start(out=wTd, in_=wT32[:])),
                waits=[("VE", ve_wt), ("GR", gr_wrow)], incs=[("OWT", 16)])
        ow_wt = em.c("OWT")
        if "wsm" in dbg:
            em.emit("gpsimd", (lambda eng, t=t: eng.dma_start(
                out=dbg["wsm"][:, t * T:(t + 1) * T], in_=wT32[:])),
                waits=[("VE", ve_wt)], incs=[("OWT", 16)])
        for q_ in range(3):
            w_ = [("OWT", ow_wt)]
            if q_ == 0 and t > 0:
                w_.append(("MM", mm_wrep_prev))
            em.emit("gpsimd", (lambda eng, q_=q_: eng.dma_start(
                out=wrow[WROW_BASE[q_] * 1:WROW_BASE[q_] + 1, :]
                if False else wrow[WROW_BASE[q_]:WROW_BASE[q_] + 1, :],
                in_=wTd.rearrange("(q j) l -> q (j l)", q=4)[q_:q_ + 1, :])),
                waits=w_, incs=[("GR", 16)])
        gr_wrow = em.c("GR")
        gr_reload = gr_wrow + 16

        if stop < 3:
            emit_dmas()
            continue
        rfree["stats_tx"] = ("VE", em.c("VE"))
        rfree["stats_tq"] = ("VE", em.c("VE"))
        if t + 1 < NTr:
            def xrd_t(c, t=t):
                return [x_plan[t + 1]]
            emit_layernorm(t, xb[(t + 1) % 2][:], xn[:], xrd_t,
                           stats=("stats_tx", "stats_tq"),
                           bc=("bct_mean", "bct_rstd"), width=KTAP, xoff=0,
                           xn_off=T, do_split=False)
        else:
            for c in range(NC):
                ev((lambda eng, c=c: eng.memset(
                    xn[:, c * (T + KTAP) + T:(c + 1) * (T + KTAP)], 0.0)))

        if stop < 4:
            emit_dmas()
            continue
        sil_ac = {}
        v2_ve = {}
        hm_ve = {}
        lo_ve = {}
        mix_ve = {}
        ve_xn_all = em.c("VE")
        for o in range(NH):
            slot, ldv = sc_info[o]
            a1 = "accA" if o % 2 == 0 else "accB"
            a2 = "accA2" if o % 2 == 0 else "accB2"
            wsl = sl_view("SC", slot, 0, 2 * D, F16)
            lsl = sl_view("SC", slot, 4 * KB, 2 * D, F16)
            w_ = [ldv, ("VE", ve_xn)] + reg_w(a1, a2)
            for k in range(NC):
                mm(a1, wsl[:, k * 128:(k + 1) * 128], xnhi[:, k * T:(k + 1) * T],
                   k == 0, False, waits=w_ if k == 0 else ())
                mm(a1, wsl[:, k * 128:(k + 1) * 128], xnlo[:, k * T:(k + 1) * T],
                   False, False)
                mm(a1, lsl[:, k * 128:(k + 1) * 128], xnhi[:, k * T:(k + 1) * T],
                   False, k == NC - 1)
            for k in range(NC):
                c0 = D + k * 128
                mm(a2, wsl[:, c0:c0 + 128], xnhi[:, k * T:(k + 1) * T],
                   k == 0, False)
                mm(a2, wsl[:, c0:c0 + 128], xnlo[:, k * T:(k + 1) * T], False, False)
                mm(a2, lsl[:, c0:c0 + 128], xnhi[:, k * T:(k + 1) * T],
                   False, k == NC - 1)
            mm_g = mm_inc()
            mark_consumed("SC")
            d_ = o
            wreg = "wrepA" if d_ % 2 == 0 else "wrepB"
            q_, j_ = d_ // 8, d_ % 8
            b_ = WROW_BASE[q_]
            w_wr = [("GR", gr_reload if q_ == 3 else gr_wrow)] + reg_w(wreg)
            if d_ < 2:
                w_wr.append(("VE", ve_xn_all))
            mm(wreg, onesP[b_:b_ + 1, 0:128],
               wrow[b_:b_ + 1, j_ * T:(j_ + 1) * T],
               True, True, waits=w_wr)
            mm_wr = mm_inc()
            if d_ == 7:
                mm_wrep_d7 = em.c("MM")
            slA = scrA[:, (o % 2) * T:(o % 2) * T + T]
            slB = scrB[:, (o % 2) * T:(o % 2) * T + T]
            slC = scrC[:, (o % 2) * T:(o % 2) * T + T]
            wa = [("MM", mm_g)]
            if o >= 2:
                wa.append(("VE", hm_ve[o - 2]))
            sil_ac[o] = ea((lambda eng, slA=slA, a1=a1, o=o: eng.activation(
                slA, REG[a1], AF.Silu, scale=w1s_s[:, o:o + 1])), waits=wa)
            wv2 = [("MM", mm_g)]
            if o >= 2:
                wv2.append(("VE", lo_ve[o - 2]))
            v2_ve[o] = ev((lambda eng, slB=slB, a2=a2, o=o: eng.tensor_scalar(
                slB, REG[a2], w2s_s[:, o:o + 1], None, ALU.mult)), waits=wv2)
            hm_ve[o] = ev((lambda eng, slC=slC, slA=slA, slB=slB: eng.tensor_tensor(
                slC, slA, slB, ALU.mult)), waits=[("AC", sil_ac[o])])
            w_hs = list(ow_splits_final) if (t == 0 and o == 0) else []
            ev((lambda eng, o=o, slC=slC: eng.tensor_copy(
                hs_hi[:, o * T:(o + 1) * T], slC)), waits=w_hs)
            lo_ve[o] = ev((lambda eng, o=o, slC=slC: eng.tensor_sub(
                hs_lo[:, o * T:(o + 1) * T], slC,
                hs_hi[:, o * T:(o + 1) * T])))
            rfree[a1] = ("AC", sil_ac[o])
            rfree[a2] = ("VE", v2_ve[o])
            wmix = [("MM", mm_wr)]
            for c in range(NC):
                base = c * (T + KTAP) + d_
                sF = scrF[:, (c % 2) * T:(c % 2) * T + T]
                ev((lambda eng, sF=sF, base=base, wreg=wreg: eng.tensor_tensor(
                    sF, REG[wreg], xn[:, base:base + T], ALU.mult)),
                   waits=wmix if c == 0 else ())
                if d_ == 0:
                    ev((lambda eng, c=c, sF=sF: eng.tensor_copy(
                        mix[:, c * T:(c + 1) * T], sF)))
                else:
                    ev((lambda eng, c=c, sF=sF: eng.tensor_add(
                        mix[:, c * T:(c + 1) * T], mix[:, c * T:(c + 1) * T], sF)))
            mix_ve[d_] = em.c("VE")
            rfree[wreg] = ("VE", mix_ve[d_])
        ve_hs = lo_ve[NH - 1]
        ve_mix = mix_ve[KTAP - 1]
        mm_wrep_prev = em.c("MM")
        em.emit("gpsimd", (lambda eng: eng.dma_start(
            out=wrow[0:1, :],
            in_=wTd.rearrange("(q j) l -> q (j l)", q=4)[3:4, :])),
            waits=[("MM", mm_wrep_d7)], incs=[("GR", 16)])
        assert em.c("GR") == gr_reload
        dump_fm("mix", fm3(mix[:]), t, wait_ve=ve_mix)
        dump_fm("hs", fm3(hs_hi), t, wait_ve=ve_hs)

        def hmat(arena, info_it, rhs_hi, rhs_lo, nout, kk, rhs_ready, scales,
                 act_fn, out_hi, out_lo, kelem):
            g_ac = {}
            g_lo = {}
            for o in range(nout):
                slot, ldv = next(info_it)
                wsl = sl_view(arena, slot, 0, kelem, F16)
                lsl = sl_view(arena, slot, 8 * KB, kelem, F16)
                a1 = "accA" if o % 2 == 0 else "accB"
                w_ = [ldv, ("VE", rhs_ready)] + reg_w(a1)
                for k in range(kk):
                    mm(a1, wsl[:, k * 128:(k + 1) * 128],
                       rhs_hi[:, k * T:(k + 1) * T], k == 0, False,
                       waits=w_ if k == 0 else ())
                    mm(a1, wsl[:, k * 128:(k + 1) * 128],
                       rhs_lo[:, k * T:(k + 1) * T], False, False)
                    mm(a1, lsl[:, k * 128:(k + 1) * 128],
                       rhs_hi[:, k * T:(k + 1) * T], False, k == kk - 1)
                mm_g = mm_inc()
                mark_consumed(arena)
                slA = scrA[:, (o % 2) * T:(o % 2) * T + T]
                wa = [("MM", mm_g)]
                if o >= 2:
                    wa.append(("VE", g_lo[o - 2]))
                g_ac[o] = ea((lambda eng, slA=slA, a1=a1, o=o: eng.activation(
                    slA, REG[a1], act_fn, scale=scales[:, o:o + 1])), waits=wa)
                rfree[a1] = ("AC", g_ac[o])
                ev((lambda eng, o=o, slA=slA: eng.tensor_copy(
                    out_hi[:, o * T:(o + 1) * T], slA)), waits=[("AC", g_ac[o])])
                g_lo[o] = ev((lambda eng, o=o, slA=slA: eng.tensor_sub(
                    out_lo[:, o * T:(o + 1) * T], slA,
                    out_hi[:, o * T:(o + 1) * T])))
            return g_lo[nout - 1]

        if stop < 5:
            emit_dmas()
            continue
        ve_g1 = hmat("SA", sa_i, hs_hi, hs_lo, NH, NH, ve_hs, h1s_s, AF.Gelu,
                     g1_hi, g1_lo, HID)
        dump_fm("g1", fm3(g1_hi), t, wait_ve=ve_g1)
        if stop < 6:
            emit_dmas()
            continue
        ve_g2 = hmat("SB", iter(sb_info), g1_hi, g1_lo, NH, NH, ve_g1, h2s_s,
                     AF.Gelu, hs_hi, hs_lo, HID)
        dump_fm("g2", fm3(hs_hi), t, wait_ve=ve_g2)

        if stop < 7:
            emit_dmas()
            continue
        w3_ve = {}
        for o in range(NC):
            slot, ldv = next(sa_i)
            wsl = sl_view("SA", slot, 0, HID, F16)
            lsl = sl_view("SA", slot, 8 * KB, HID, F16)
            a1 = "accA" if o % 2 == 0 else "accB"
            w_ = [ldv, ("VE", ve_g2)] + reg_w(a1)
            for k in range(NH):
                mm(a1, wsl[:, k * 128:(k + 1) * 128], hs_hi[:, k * T:(k + 1) * T],
                   k == 0, False, waits=w_ if k == 0 else ())
                mm(a1, wsl[:, k * 128:(k + 1) * 128], hs_lo[:, k * T:(k + 1) * T],
                   False, False)
                mm(a1, lsl[:, k * 128:(k + 1) * 128], hs_hi[:, k * T:(k + 1) * T],
                   False, k == NH - 1)
            mm_g = mm_inc()
            mark_consumed("SA")
            slB = scrB[:, (o % 2) * T:(o % 2) * T + T]
            ev((lambda eng, slB=slB, a1=a1, o=o, t=t: eng.scalar_tensor_tensor(
                slB, REG[a1], w3s_s[:, o:o + 1],
                xb[t % 2][:, o * T:(o + 1) * T], ALU.mult, ALU.add)),
               waits=[("MM", mm_g), ("VE", ve_mix)])
            w3_ve[o] = ev((lambda eng, slB=slB, o=o, t=t: eng.tensor_add(
                xb[t % 2][:, o * T:(o + 1) * T], slB,
                mix[:, o * T:(o + 1) * T])))
            rfree[a1] = ("VE", w3_ve[o])
        dump_fm("x1", fm3(xb[t % 2][:]), t, wait_ve=w3_ve[NC - 1])

        if stop < 8:
            emit_dmas()
            continue
        def xrd2(c):
            return [("VE", w3_ve[c])]
        _, xn2_split = emit_layernorm(t, xb[t % 2][:], xn[:], xrd2)
        ve_xn2 = xn2_split[-1]
        dump_fm("xn2", fm3(xn[:], w=T + KTAP)[:, :, 0:T], t, wait_ve=ve_xn2)

        if stop < 9:
            emit_dmas()
            continue
        hh_ve = {}
        cz_ac = {}
        for oc in range(2 * NCH):
            slot, ldv = next(sa_i)
            wsl = sl_view("SA", slot, 0, D, F16)
            lsl = sl_view("SA", slot, 2 * KB, D, F16)
            a1 = "accA" if oc % 2 == 0 else "accB"
            w_ = [ldv, ("VE", ve_xn2)] + reg_w(a1)
            for k in range(NC):
                mm(a1, wsl[:, k * 128:(k + 1) * 128], xnhi[:, k * T:(k + 1) * T],
                   k == 0, False, waits=w_ if k == 0 else ())
                mm(a1, wsl[:, k * 128:(k + 1) * 128], xnlo[:, k * T:(k + 1) * T],
                   False, False)
                mm(a1, lsl[:, k * 128:(k + 1) * 128], xnhi[:, k * T:(k + 1) * T],
                   False, k == NC - 1)
            mm_g = mm_inc()
            mark_consumed("SA")
            slA = scrA[:, (oc % 2) * T:(oc % 2) * T + T]
            slB = scrB[:, (oc % 2) * T:(oc % 2) * T + T]
            slC = scrD[:, (oc % 2) * T:(oc % 2) * T + T]
            wv = [("MM", mm_g)]
            if oc >= 2:
                wv.append(("VE", hh_ve[oc - 2]))
                wv.append(("AC", cz_ac[oc - 2]))
            ve_hb = ev((lambda eng, slA=slA, a1=a1, oc=oc: eng.tensor_scalar(
                slA, REG[a1], bhg_s[:, oc:oc + 1], None, ALU.add)), waits=wv)
            rfree[a1] = ("VE", ve_hb)
            if oc < NCH:
                ev((lambda eng, slB=slB, slA=slA: eng.tensor_single_scalar(
                    slB, slA, 0.5, ALU.add)))
                ev((lambda eng, slC=slC, slA=slA: eng.tensor_scalar(
                    slC, slA, 0.0, 0.0, ALU.add, ALU.is_ge)))
                ac_s = ea((lambda eng, oc=oc, slA=slA: eng.activation(
                    hhat[:, oc * T:(oc + 1) * T], slA, AF.Sigmoid)),
                    waits=[("VE", em.c("VE"))])
                cz_ac[oc] = ac_s
                ev((lambda eng, oc=oc, slB=slB: eng.tensor_sub(
                    slB, slB, hhat[:, oc * T:(oc + 1) * T])),
                   waits=[("AC", ac_s)])
                ev((lambda eng, slB=slB, slC=slC: eng.tensor_mul(slB, slC, slB)))
                hh_ve[oc] = ev((lambda eng, oc=oc, slB=slB:
                                eng.tensor_add(hhat[:, oc * T:(oc + 1) * T],
                                               hhat[:, oc * T:(oc + 1) * T], slB)))
            else:
                j = oc - NCH
                ac_c = ea((lambda eng, j=j, slA=slA: eng.activation(
                    c_sb[:, j * T:(j + 1) * T], slA, AF.Sigmoid, scale=-1.0)),
                    waits=[("VE", ve_hb)])
                ac_z = ea((lambda eng, slB=slB, slA=slA: eng.activation(
                    slB, slA, AF.Sigmoid)))
                cz_ac[oc] = ac_z
                hh_ve[oc] = ev((lambda eng, j=j, slB=slB: eng.tensor_mul(
                    v_sb[:, j * T:(j + 1) * T], slB,
                    hhat[:, j * T:(j + 1) * T])), waits=[("AC", ac_z)])
        ve_gates = hh_ve[2 * NCH - 1]
        ac_gates = cz_ac[2 * NCH - 1]
        dump_fm("cc", fm3(c_sb), t, wait_ac=ac_gates)
        dump_fm("vv", fm3(v_sb), t, wait_ve=ve_gates)

        if stop < 10:
            emit_dmas()
            continue
        for j in range(NCH):
            ve_sc = ev((lambda eng, j=j: eng.tensor_tensor_scan(
                ho[:, j * T:(j + 1) * T], c_sb[:, j * T:(j + 1) * T],
                v_sb[:, j * T:(j + 1) * T], st[:, j:j + 1],
                ALU.mult, ALU.add)),
               waits=[("AC", ac_gates), ("VE", ve_gates)] if j == 0 else ())
            ev((lambda eng, j=j: eng.tensor_copy(
                st[:, j:j + 1], ho[:, j * T + T - 1:j * T + T])),
               waits=[("VE", ve_sc)])
        ve_scans = em.c("VE")
        for j in range(NCH):
            ev((lambda eng, j=j: eng.tensor_copy(
                ho_hi[:, j * T:(j + 1) * T], ho[:, j * T:(j + 1) * T])),
               waits=[("VE", ve_scans)] if j == 0 else ())
            ve_ho = ev((lambda eng, j=j: eng.tensor_sub(
                ho_lo[:, j * T:(j + 1) * T], ho[:, j * T:(j + 1) * T],
                ho_hi[:, j * T:(j + 1) * T])))
        dump_fm("ho", fm3(ho), t, wait_ve=ve_ho)

        if stop < 11:
            emit_dmas()
            continue
        acc_ve = {}
        for o in range(NC):
            slot, ldv = next(sa_i)
            wsl = sl_view("SA", slot, 0, CH, F16)
            lsl = sl_view("SA", slot, 3 * KB, CH, F16)
            a1 = "accA" if o % 2 == 0 else "accB"
            w_ = [ldv, ("VE", ve_ho)] + reg_w(a1)
            for k in range(NCH):
                mm(a1, wsl[:, k * 128:(k + 1) * 128], ho_hi[:, k * T:(k + 1) * T],
                   k == 0, False, waits=w_ if k == 0 else ())
                mm(a1, wsl[:, k * 128:(k + 1) * 128], ho_lo[:, k * T:(k + 1) * T],
                   False, False)
                mm(a1, lsl[:, k * 128:(k + 1) * 128], ho_hi[:, k * T:(k + 1) * T],
                   False, k == NCH - 1)
            mm_g = mm_inc()
            mark_consumed("SA")
            acc_ve[o] = ev((lambda eng, a1=a1, o=o, t=t: eng.scalar_tensor_tensor(
                acc2[:, o * T:(o + 1) * T], REG[a1], bout_s[:, o:o + 1],
                xb[t % 2][:, o * T:(o + 1) * T], ALU.add, ALU.add)),
                waits=[("MM", mm_g)])
            rfree[a1] = ("VE", acc_ve[o])
        dump_fm("acc2", fm3(acc2), t, wait_ve=acc_ve[NC - 1])

        if stop < 12:
            emit_dmas()
            continue
        tp_ve = {}
        for m_ in range(NTOK):
            for c in range(NC):
                r = "tpA" if (m_ * NC + c) % 2 == 0 else "tpB"
                in_ap = xn[:, c * (T + KTAP) + m_ * 128:c * (T + KTAP) + m_ * 128 + 128]
                mmw = tp(r, in_ap, waits=[("VE", ve_xn2)] + reg_w(r))
                tp_ve[(m_, c)] = ev((lambda eng, m_=m_, c=c, r=r: eng.tensor_copy(
                    xtm[:, m_ * D + c * 128:m_ * D + (c + 1) * 128], REG[r])),
                    waits=[("MM", mmw)])
                reg_done(r)
        q_ve = {}
        for m_ in range(NTOK):
            q_ve[m_] = emit_quant(xtm[:, m_ * D:(m_ + 1) * D],
                                  xd_tm[:, m_ * D:(m_ + 1) * D],
                                  q1, q2, q3, q4,
                                  waits=[("VE", tp_ve[(m_, NC - 1)])])
        xd_ve = {}
        for m_ in range(NTOK):
            for c in range(NC):
                r = "tpA" if (m_ * NC + c) % 2 == 0 else "tpB"
                in_ap = xd_tm[:, m_ * D + c * 128:m_ * D + (c + 1) * 128]
                mmw = tp(r, in_ap, waits=[("VE", q_ve[m_])] + reg_w(r))
                xd_ve[(m_, c)] = ev((lambda eng, m_=m_, c=c, r=r: eng.tensor_copy(
                    xd_fm[:, c * T + m_ * 128:c * T + m_ * 128 + 128], REG[r])),
                    waits=[("MM", mmw)])
                reg_done(r)
        ve_xd = xd_ve[(NTOK - 1, NC - 1)]
        dump_fm("xd", fm3(xd_fm), t, wait_ve=ve_xd)

        fin_ve = {}
        for o in range(NC):
            slot, ldv = next(sa_i)
            wsl = sl_view("SA", slot, 0, D, F32)
            a1 = "accA" if o % 2 == 0 else "accB"
            w_ = [ldv, ("VE", ve_xd)] + reg_w(a1)
            for k in range(NC):
                mm(a1, wsl[:, k * 128:(k + 1) * 128], xd_fm[:, k * T:(k + 1) * T],
                   k == 0, k == NC - 1, waits=w_ if k == 0 else ())
            mm_g = mm_inc()
            mark_consumed("SA")
            fin_ve[o] = ev((lambda eng, a1=a1, o=o, t=t: eng.scalar_tensor_tensor(
                xb[t % 2][:, o * T:(o + 1) * T], REG[a1], l1b_s[:, o:o + 1],
                acc2[:, o * T:(o + 1) * T], ALU.add, ALU.add)),
                waits=[("MM", mm_g), ("VE", acc_ve[o])])
            rfree[a1] = ("VE", fin_ve[o])
        ve_final = fin_ve[NC - 1]

        osem = "OWO" + str(t % 2)
        for c in range(NC):
            em.emit("gpsimd", (lambda eng, t=t, c=c: eng.dma_start(
                out=outT[c * 128:(c + 1) * 128, t * T:(t + 1) * T],
                in_=xb[t % 2][:, c * T:(c + 1) * T])),
                waits=[("VE", ve_final)] if c == 0 else (),
                incs=[(osem, 16)])
        out_ow[t] = (osem, em.c(osem))

        emit_dmas()

    fin_waits = [("OWO0", em.c("OWO0")), ("OWO1", em.c("OWO1")),
                 ("OWT", em.c("OWT")), ("GR", em.c("GR")),
                 ("OWS0", em.c("OWS0")), ("OWS1", em.c("OWS1"))]
    em.emit("gpsimd", None, waits=fin_waits)
    em.emit("sync", None, waits=fin_waits)
    em.check()

    sem_handles = {}
    import contextlib
    stack = contextlib.ExitStack()
    for s in em.ctr:
        sem_handles[s] = stack.enter_context(nc.semaphore(s))
    with stack:
        with nc.Block() as block:
            def mk_body(ename):
                entries = em.ops[ename]
                self_ser = ename == "vector"

                def body(eng, entries=entries, self_ser=self_ser):
                    prev = 0
                    for waits, fn, incs in entries:
                        wmax = prev if self_ser else 0
                        for s, v in waits:
                            if self_ser and s == "VE":
                                wmax = max(wmax, v)
                            else:
                                eng.wait_ge(sem_handles[s], v)
                        if self_ser and wmax > 0:
                            eng.wait_ge(sem_handles["VE"], wmax)
                        if fn is None:
                            continue
                        ins = fn(eng)
                        for s, n in incs:
                            ins.then_inc(sem_handles[s], n)
                            if self_ser and s == "VE":
                                prev += n
                return body

            block.sync(mk_body("sync"))
            block.tensor(mk_body("tensor"))
            block.vector(mk_body("vector"))
            block.scalar(mk_body("scalar"))
            block.gpsimd(mk_body("gpsimd"))

    meta = {"sbuf_left": sbuf_left, "n_ops": {e: len(em.ops[e]) for e in ENGINES}}
    return nc, meta


def _pretile(w):
    O, I = w.shape
    oc, kc = O // 128, I // 128
    r = w.reshape(oc, 128, kc, 128).transpose(0, 3, 2, 1)
    return np.ascontiguousarray(r.reshape(oc, 128, I))


def _cols(v):
    n = v.shape[0] // 128
    return np.ascontiguousarray(v.reshape(n, 128).T)


_prog_cache = {}


def kernel(L0_attn_x, L0_attn_W, L0_attn_b, L0_ffn_w1_w, L0_ffn_w1_s,
           L0_ffn_w2_w, L0_ffn_w2_s, L0_ffn_h1_w, L0_ffn_h1_s, L0_ffn_h2_w,
           L0_ffn_h2_s, L0_ffn_w3_w, L0_ffn_w3_s, L1_attn_W_hg, L1_attn_b_hg,
           L1_attn_W_out, L1_attn_b_out, L1_attn_h0, L1_ffn_weight, L1_ffn_bias,
           _debug=(), _trace=False, _stop=12):
    f32 = np.float32
    x = np.asarray(L0_attn_x, f32)
    B = x.shape[0]

    w1 = np.asarray(L0_ffn_w1_w, f32)
    w2 = np.asarray(L0_ffn_w2_w, f32)
    w12 = np.concatenate([_pretile(w1), _pretile(w2)], axis=2)
    shared = {
        "w12t": w12,
        "h1t": _pretile(np.asarray(L0_ffn_h1_w, f32)),
        "h2t": _pretile(np.asarray(L0_ffn_h2_w, f32)),
        "w3t": _pretile(np.asarray(L0_ffn_w3_w, f32)),
        "whgt": _pretile(np.asarray(L1_attn_W_hg, f32)),
        "woutt": _pretile(np.asarray(L1_attn_W_out, f32)),
        "wlogt": np.ascontiguousarray(
            np.asarray(L0_attn_W, f32).reshape(KTAP, NC, 128).transpose(2, 1, 0)
            .reshape(128, NC * KTAP)),
        "blogr": np.ascontiguousarray(np.asarray(L0_attn_b, f32)[None, :]),
        "w1s": _cols(np.asarray(L0_ffn_w1_s, f32)),
        "w2s": _cols(np.asarray(L0_ffn_w2_s, f32)),
        "h1sc": _cols(np.asarray(L0_ffn_h1_s, f32)),
        "h2sc": _cols(np.asarray(L0_ffn_h2_s, f32)),
        "w3sc": _cols(np.asarray(L0_ffn_w3_s, f32)),
        "bhg": _cols(np.asarray(L1_attn_b_hg, f32)),
        "bout": _cols(np.asarray(L1_attn_b_out, f32)),
        "h0t": _cols(np.asarray(L1_attn_h0, f32)),
        "l1w": np.ascontiguousarray(np.asarray(L1_ffn_weight, f32)),
        "l1b": _cols(np.asarray(L1_ffn_bias, f32)),
        "eye": np.eye(128, dtype=f32),
        "onescol": np.ones((128, 1), f32),
    }

    Lr = x.shape[1]
    key = (tuple(sorted(_debug)), Lr, _stop)
    if key not in _prog_cache:
        _prog_cache[key] = build_program(debug=_debug, Lr=Lr, stop=_stop)
    nc, meta = _prog_cache[key]

    in_maps = []
    for b in range(B):
        m = dict(shared)
        m["xT"] = np.ascontiguousarray(x[b].T)
        in_maps.append(m)

    res = run_bass_kernel_spmd(nc, in_maps, core_ids=list(range(B)),
                               trace=_trace)
    out = np.stack([np.ascontiguousarray(r["outT"].T) for r in res.results])
    if _debug:
        dbgs = [{k[len("dbg_"):]: v for k, v in r.items() if k.startswith("dbg_")}
                for r in res.results]
        return out, dbgs, res
    return out
